# version 01
# baseline (speedup 1.0000x reference)
import numpy as np
import concourse.bass as bass
import concourse.mybir as mybir
from concourse.bass_utils import run_bass_kernel_spmd

F32 = mybir.dt.float32
BF16 = mybir.dt.bfloat16
I32 = mybir.dt.int32
AF = mybir.ActivationFunctionType
ALU = mybir.AluOpType
AX = mybir.AxisListType

ENGINES = ("sync", "vector", "scalar", "tensor", "gpsimd")

D = 1024
TOK = 2048
TG = 512


class Prog:

    def __init__(self, nc):
        self.nc = nc
        self.B = nc.alloc_semaphore(name="Bar")
        self.btot = 0
        self.ops = []
        self.dirty = {e: None for e in ENGINES}
        self.pending_wait = {e: None for e in ENGINES}
        self._sems = {}

    def sem(self, name):
        if name not in self._sems:
            self._sems[name] = self.nc.alloc_semaphore(name=name)
        return self._sems[name]

    def op(self, eng, fn, dma=False, wait=None, inc=None):
        entry = {"eng": eng, "fn": fn, "waits": list(wait) if wait else [],
                 "incs": list(inc) if inc else [], "dma": dma}
        if self.pending_wait[eng] is not None:
            entry["waits"].insert(0, (self.B, self.pending_wait[eng]))
            self.pending_wait[eng] = None
        if dma:
            entry["incs"].append((self.B, 16))
            self.btot += 16
        self.ops.append(entry)
        self.dirty[eng] = len(self.ops) - 1
        return entry

    def dma(self, eng, out, in_, **kw):
        return self.op(eng, lambda e, o=out, i=in_: e.dma_start(out=o, in_=i, **kw),
                       dma=True)

    def dma3(self, eng, out3, in3, max_chunks=4, **kw):
        Cn = out3.shape[1]
        for a in range(0, Cn, max_chunks):
            b = min(a + max_chunks, Cn)
            self.dma(eng, out3[:, a:b, :], in3[:, a:b, :], **kw)

    def cc(self, kind, op, replica_groups, ins, outs, wait=None):
        return self.op("gpsimd", lambda e: e.collective_compute(
            kind, op, replica_groups=replica_groups, ins=ins, outs=outs),
            dma=True, wait=wait)

    def barrier(self):
        for e in ENGINES:
            i = self.dirty[e]
            if i is not None and not self.ops[i]["dma"]:
                self.ops[i]["incs"].append((self.B, 1))
                self.btot += 1
            self.dirty[e] = None
        for e in ENGINES:
            self.pending_wait[e] = self.btot

    def final_wait(self, engines=("sync", "gpsimd")):
        for e in engines:
            v = self.pending_wait[e] if self.pending_wait[e] is not None else self.btot
            self.ops.append({"eng": e, "fn": None, "waits": [(self.B, v)],
                             "incs": [], "dma": False})
            self.pending_wait[e] = None

    def emit(self):
        nc = self.nc
        per = {e: [x for x in self.ops if x["eng"] == e] for e in ENGINES}
        with nc.Block() as blk:
            for e in ENGINES:
                entries = per[e]
                if not entries:
                    continue

                def body(eng, entries=entries):
                    for x in entries:
                        for (s, v) in x["waits"]:
                            eng.wait_ge(s, v)
                        if x["fn"] is not None:
                            inst = x["fn"](eng)
                            for (s, n) in x["incs"]:
                                inst.then_inc(s, n)

                getattr(blk, e)(body)



class Ctx:
    def __init__(self, nc):
        self.nc = nc
        self.P = Prog(nc)
        a = nc.alloc_sbuf_tensor
        self.XB = a("XB", [128, 16384], F32).ap()
        self.YB = a("YB", [128, 16384], BF16).ap()
        self.ZB = a("ZB", [128, 16384], BF16).ap()
        self.WB = a("WB", [128, 4096], BF16).ap()
        self.WF = a("WF", [128, 4096], F32).ap()
        self.SM = a("SM", [128, 2048], F32).ap()
        self.RW = a("RW", [1, 8192], F32).ap()
        self.IDB = a("IDB", [128, 128], BF16).ap()
        self.IDF = a("IDF", [128, 128], F32).ap()
        self.AUX = a("AUX", [128, 512], F32).ap()
        self.AUX2 = a("AUX2", [128, 512], F32).ap()
        self.MSK = a("MSK", [128, 128], F32).ap()
        self.CONST = a("CONST", [128, 8], F32).ap()
        self.CONSTB = a("CONSTB", [128, 4], BF16).ap()
        self.PS = [nc.alloc_psum_tensor(f"PS{i}", [128, 512], F32).ap()
                   for i in range(7)]
        self.PT = nc.alloc_psum_tensor("PT", [128, 1024], BF16).ap()
        self.ROWD = nc.dram_tensor("ROWD", [1, 8192], F32, kind="Internal").ap()

    def emit_identities(self):
        P = self.P
        for t in (self.IDB, self.IDF):
            P.op("gpsimd", lambda e, t=t: e.memset(t, 1.0))
        P.op("gpsimd", lambda e: e.memset(self.CONST[:, 1:2], 1.0))
        P.op("gpsimd", lambda e: e.memset(self.CONSTB[:, 1:2], 1.0))
        P.barrier()
        for t in (self.IDB, self.IDF):
            P.op("gpsimd", lambda e, t=t: e.affine_select(
                out=t, in_=t, compare_op=ALU.is_equal, fill=0.0, base=0,
                pattern=[[-1, 128]], channel_multiplier=1))
        P.barrier()


def rearr_fm(d_ap, c0, c1, t0, t1):
    return d_ap[c0 * 128:c1 * 128, t0:t1].rearrange("(c p) t -> p c t", p=128)


def sb3(sb_ap, nchunk, tlen):
    return sb_ap.rearrange("p (c t) -> p c t", c=nchunk)



def emit_ln(C, x_d, T, dst_d, eps=1e-5, gamma_d=None, beta_d=None,
            keep_half=None, dst_sb=True):
    P = C.P
    ngrp = T // 2048
    order = list(range(ngrp))
    if keep_half is not None and ngrp == 2:
        order = [1 - keep_half, keep_half]
    if gamma_d is not None:
        P.dma("sync", C.AUX[:, 0:8], gamma_d.rearrange("(c p) -> p c", p=128),
              allow_slow_non_contiguous=True)
        P.dma("sync", C.AUX2[:, 0:8], beta_d.rearrange("(c p) -> p c", p=128),
              allow_slow_non_contiguous=True)
        P.barrier()
    for g in order:
        t0 = g * 2048
        P.dma3("sync", sb3(C.XB[:, 0:8 * 2048], 8, 2048), rearr_fm(x_d, 0, 8, t0, t0 + 2048))
        P.barrier()
        emit_ln_core(C, lambda c: C.XB[:, c * 2048:(c + 1) * 2048], 2048,
                     lambda c: C.YB[:, c * 2048:(c + 1) * 2048], eps,
                     use_gamma=(gamma_d is not None))
        if dst_d is not None:
            P.dma3("sync", rearr_fm(dst_d, 0, 8, t0, t0 + 2048), sb3(C.YB[:, 0:8 * 2048], 8, 2048))
            P.barrier()


def emit_ln_core(C, src_c, T2, dst_c, eps, use_gamma=False):
    P = C.P
    nj = T2 // 512
    assert T2 <= 2048
    for j in range(nj):
        for c in range(8):
            ones = C.CONST[:, 1:2] if src_c(0).dtype == F32 else C.CONSTB[:, 1:2]
            P.op("tensor", lambda e, j=j, c=c, ones=ones: e.matmul(
                C.PS[j][0:1, :], ones,
                src_c(c)[:, j * 512:(j + 1) * 512],
                start=(c == 0), stop=(c == 7)))
    P.barrier()
    for j in range(nj):
        P.op("vector", lambda e, j=j: e.tensor_copy(
            C.RW[0:1, j * 512:(j + 1) * 512], C.PS[j][0:1, :]))
    P.barrier()
    for j in range(nj):
        for c in range(8):
            P.op("scalar", lambda e, c=c, j=j: e.activation(
                C.WF[:, 0:512], src_c(c)[:, j * 512:(j + 1) * 512], AF.Square))
            P.barrier()
            P.op("tensor", lambda e, j=j, c=c: e.matmul(
                C.PS[j][0:1, :], C.CONST[:, 1:2], C.WF[:, 0:512],
                start=(c == 0), stop=(c == 7)))
            P.barrier()
    for j in range(nj):
        P.op("vector", lambda e, j=j: e.tensor_copy(
            C.RW[0:1, 2048 + j * 512: 2048 + (j + 1) * 512], C.PS[j][0:1, :]))
    P.barrier()
    mu = C.RW[0:1, 0:T2]
    sq = C.RW[0:1, 2048:2048 + T2]
    var = C.RW[0:1, 4096:4096 + T2]
    rstd = C.RW[0:1, 6144:6144 + T2]
    P.op("vector", lambda e: e.tensor_scalar_mul(mu, mu, 1.0 / 1024.0))
    P.barrier()
    P.op("vector", lambda e: e.tensor_mul(var, mu, mu))
    P.barrier()
    P.op("vector", lambda e: e.scalar_tensor_tensor(
        var, sq, 1.0 / 1024.0, var, ALU.mult, ALU.subtract))
    P.barrier()
    P.op("gpsimd", lambda e: e.memset(C.CONST[:, 0:1], float(eps)))
    P.barrier()
    P.op("scalar", lambda e: e.activation(rstd, var, AF.Sqrt,
                                          bias=C.CONST[0:1, 0:1]))
    P.barrier()
    P.op("vector", lambda e: e.reciprocal(rstd, rstd))
    P.barrier()
    P.op("vector", lambda e: e.tensor_mul(var, mu, rstd))
    P.barrier()
    P.dma("sync", C.ROWD[0:1, 0:T2], rstd)
    P.dma("sync", C.ROWD[0:1, 2048:2048 + T2], var)
    P.barrier()
    P.dma("sync", C.SM[:, 0:T2], C.ROWD[0:1, 0:T2].broadcast_to([128, T2]))
    P.dma("sync", C.WF[:, 2048:2048 + T2],
          C.ROWD[0:1, 2048:2048 + T2].broadcast_to([128, T2]))
    P.barrier()
    for c in range(8):
        P.op("vector", lambda e, c=c: e.tensor_mul(
            C.WF[:, 0:T2], src_c(c), C.SM[:, 0:T2]))
        P.barrier()
        if use_gamma:
            P.op("vector", lambda e: e.tensor_sub(
                C.WF[:, 0:T2], C.WF[:, 0:T2], C.WF[:, 2048:2048 + T2]))
            P.barrier()
            P.op("vector", lambda e, c=c: e.tensor_scalar(
                dst_c(c), C.WF[:, 0:T2], C.AUX[:, c:c + 1], C.AUX2[:, c:c + 1],
                ALU.mult, ALU.add))
            P.barrier()
        else:
            P.op("vector", lambda e, c=c: e.tensor_sub(
                dst_c(c), C.WF[:, 0:T2], C.WF[:, 2048:2048 + T2]))
            P.barrier()



def emit_proj(C, w_d, IF, OF, T, src_of, evac, stage_tg=None, wdtype="f32",
              w_cols=None):
    P = C.P
    nic = IF // 128
    noc = OF // 128
    ntg = T // TG
    for ofc in range(noc):
        c0 = ofc * 128 if w_cols is None else w_cols(ofc)
        wsl = w_d[:, c0:c0 + 128].rearrange("(c p) o -> p c o", p=128)
        if wdtype == "bf16":
            P.dma3("sync", sb3(C.WB[:, 0:nic * 128], nic, 128), wsl)
            P.barrier()
        else:
            dst = C.WF[:, 0:nic * 128]
            if wdtype == "i32":
                dst = dst.bitcast(I32)
            P.dma3("sync", sb3(dst, nic, 128), wsl)
            P.barrier()
            P.op("vector", lambda e, n=nic, d=dst: e.tensor_copy(
                C.WB[:, 0:n * 128], d))
            P.barrier()
        for tg in range(ntg):
            if stage_tg is not None:
                stage_tg(tg)
            ps = C.PS[tg % 2]
            for ic in range(nic):
                src = src_of(ic, tg)
                P.op("tensor", lambda e, ic=ic, ps=ps, src=src: e.matmul(
                    ps, C.WB[:, ic * 128:(ic + 1) * 128], src,
                    start=(ic == 0), stop=(ic == nic - 1)))
            P.barrier()
            evac(ofc, tg, ps)
            P.barrier()



def build_l1f():
    nc = bass.Bass("TRN2", target_bir_lowering=False)
    x_d = nc.dram_tensor("xi", [D, TOK], F32, kind="ExternalInput").ap()
    wg_d = nc.dram_tensor("wg", [D, 3072], I32, kind="ExternalInput").ap()
    wu_d = nc.dram_tensor("wu", [D, 3072], I32, kind="ExternalInput").ap()
    wd_d = nc.dram_tensor("wd", [3072, D], I32, kind="ExternalInput").ap()
    xo_d = nc.dram_tensor("xo", [D, TOK], F32, kind="ExternalOutput").ap()
    h_d = nc.dram_tensor("h_d", [3072, TOK], BF16, kind="Internal").ap()

    C = Ctx(nc)
    P = C.P
    C.emit_identities()
    emit_ln(C, x_d, TOK, None)

    nic, noc, ntg = 8, 24, 4
    for ofc in range(noc):
        for (wi, w_ap) in ((0, wg_d), (1, wu_d)):
            wsl = w_ap[:, ofc * 128:(ofc + 1) * 128].rearrange(
                "(c p) o -> p c o", p=128)
            P.dma3("sync", sb3(C.WF[:, 0:nic * 128].bitcast(I32), nic, 128), wsl)
            P.barrier()
            P.op("vector", lambda e, wi=wi: e.tensor_copy(
                C.WB[:, wi * 1024:(wi + 1) * 1024],
                C.WF[:, 0:nic * 128].bitcast(I32)))
            P.barrier()
        for tg in range(ntg):
            for (wi, ps) in ((0, C.PS[0]), (1, C.PS[1])):
                for ic in range(nic):
                    P.op("tensor", lambda e, wi=wi, ic=ic, ps=ps, tg=tg: e.matmul(
                        ps, C.WB[:, wi * 1024 + ic * 128: wi * 1024 + (ic + 1) * 128],
                        C.YB[:, ic * 2048 + tg * 512: ic * 2048 + (tg + 1) * 512],
                        start=(ic == 0), stop=(ic == nic - 1)))
            P.barrier()
            P.op("scalar", lambda e: e.activation(C.AUX, C.PS[0], AF.Silu,
                                                  scale=1.0 / 127.0))
            P.barrier()
            P.op("vector", lambda e: e.scalar_tensor_tensor(
                C.ZB[:, 0:512], C.PS[1], 1.0 / 127.0, C.AUX, ALU.mult, ALU.mult))
            P.barrier()
            P.dma("sync", h_d[ofc * 128:(ofc + 1) * 128, tg * 512:(tg + 1) * 512],
                  C.ZB[:, 0:512])
            P.barrier()

    def stage_tg(tg):
        C.P.dma3("sync", sb3(C.ZB[:, 0:24 * 512], 24, 512),
                 rearr_fm(h_d, 0, 24, tg * 512, (tg + 1) * 512), max_chunks=6)
        C.P.barrier()

    def src_of(ic, tg):
        return C.ZB[:, ic * 512:(ic + 1) * 512]

    def evac(ofc, tg, ps):
        xsl = C.XB[:, ofc * 2048 + tg * 512: ofc * 2048 + (tg + 1) * 512]
        P.op("vector", lambda e, ps=ps, xsl=xsl: e.scalar_tensor_tensor(
            C.AUX, ps, 1.0 / 127.0, xsl, ALU.mult, ALU.add))
        P.barrier()
        P.dma("sync", xo_d[ofc * 128:(ofc + 1) * 128, tg * 512:(tg + 1) * 512],
              C.AUX)

    emit_proj(C, wd_d, 3072, D, TOK, src_of, evac,
              stage_tg=lambda tg: stage_tg(tg), wdtype="i32")
    P.barrier()
    P.final_wait()
    P.emit()
    return nc





def emit_qkv_tokmajor(C, src_d, w_d, wcol0, ncols, Tt, dst_cb, psbank, nic=8):
    P = C.P
    W = min(512, ncols)
    nwg = ncols // W
    for wg in range(nwg):
        P.dma3("sync", sb3(C.WB[:, 0:nic * W], nic, W),
               w_d[:, wcol0 + wg * W: wcol0 + (wg + 1) * W]
               .rearrange("(c p) o -> p c o", p=128))
        P.barrier()
        for tt in range(Tt // 128):
            P.dma3("sync", sb3(C.YB[:, 0:nic * 128], nic, 128),
                   rearr_fm(src_d, 0, nic, tt * 128, (tt + 1) * 128))
            P.barrier()
            ps = C.PS[psbank + tt % 2][:, 0:W]
            for ic in range(nic):
                P.op("tensor", lambda e, ic=ic, ps=ps: e.matmul(
                    ps, C.YB[:, ic * 128:(ic + 1) * 128],
                    C.WB[:, ic * W:(ic + 1) * W],
                    start=(ic == 0), stop=(ic == nic - 1)))
            P.barrier()
            dst_cb(wg, tt, ps)
            P.barrier()


def emit_headln_evac(C, dst_d, gb_off, wg, tt, ps, nheads_half=8):
    P = C.P
    w = C.WF[:, 0:512]
    sq = C.WF[:, 512:1024]
    mu = C.AUX[:, 80:88]
    sumsq = C.AUX[:, 88:96]
    rstd = C.AUX[:, 96:104]
    P.op("vector", lambda e, ps=ps: e.tensor_reduce(
        mu.rearrange("p (a b) -> p a b", b=1), ps.rearrange("p (h d) -> p h d", d=64),
        AX.X, ALU.add))
    P.barrier()
    P.op("scalar", lambda e, ps=ps: e.activation(sq, ps, AF.Square))
    P.barrier()
    P.op("vector", lambda e: e.tensor_reduce(
        sumsq.rearrange("p (a b) -> p a b", b=1),
        sq.rearrange("p (h d) -> p h d", d=64), AX.X, ALU.add))
    P.barrier()
    P.op("vector", lambda e: e.tensor_scalar_mul(mu, mu, 1.0 / 64.0))
    P.barrier()
    P.op("vector", lambda e: e.tensor_mul(rstd, mu, mu))
    P.barrier()
    P.op("vector", lambda e: e.scalar_tensor_tensor(
        rstd, sumsq, 1.0 / 64.0, rstd, ALU.mult, ALU.subtract))
    P.barrier()
    P.op("gpsimd", lambda e: e.memset(C.CONST[:, 0:1], 1e-5))
    P.barrier()
    P.op("scalar", lambda e: e.activation(rstd, rstd, AF.Sqrt,
                                          bias=C.CONST[:, 0:1]))
    P.barrier()
    P.op("vector", lambda e: e.reciprocal(rstd, rstd))
    P.barrier()
    for hh in range(8):
        P.op("vector", lambda e, hh=hh, ps=ps: e.tensor_scalar(
            w[:, hh * 64:(hh + 1) * 64], ps[:, hh * 64:(hh + 1) * 64],
            mu[:, hh:hh + 1], rstd[:, hh:hh + 1], ALU.subtract, ALU.mult))
    P.barrier()
    P.op("vector", lambda e: e.tensor_mul(w, w, C.SM[:, gb_off:gb_off + 512]))
    P.barrier()
    P.op("vector", lambda e: e.tensor_add(
        C.YB[:, 2048:2560], w, C.SM[:, gb_off + 512:gb_off + 1024]))
    P.barrier()
    for q in range(4):
        P.op("tensor", lambda e, q=q: e.transpose(
            C.PT[:, 0:128], C.YB[:, 2048 + q * 128:2048 + (q + 1) * 128], C.IDB))
        P.barrier()
        P.op("vector", lambda e: e.tensor_copy(C.YB[:, 2560:2688],
                                               C.PT[:, 0:128]))
        P.barrier()
        P.dma("sync", dst_d[wg * 512 + q * 128: wg * 512 + (q + 1) * 128,
                            tt * 128:(tt + 1) * 128], C.YB[:, 2560:2688])
        P.barrier()


def emit_norm_bound(C, q_d, k_d, Tk, scale):
    P = C.P
    P.op("gpsimd", lambda e: e.memset(C.RW[0:1, 0:64], 0.0))
    P.barrier()
    for (src_d, T, cell0) in ((q_d, 2048, 0), (k_d, Tk, 16)):
        for ofc in range(8):
            for tg in range(T // 512):
                P.dma("sync", C.YB[:, 0:512],
                      src_d[ofc * 128:(ofc + 1) * 128, tg * 512:(tg + 1) * 512])
                P.barrier()
                P.op("scalar", lambda e: e.activation(C.WF[:, 0:512],
                                                      C.YB[:, 0:512], AF.Square))
                P.barrier()
                for seg in range(2):
                    h = ofc * 2 + seg
                    P.op("tensor", lambda e, seg=seg: e.matmul(
                        C.PS[4][0:1, :], C.CONST[seg * 64:seg * 64 + 64, 1:2],
                        C.WF[seg * 64:seg * 64 + 64, 0:512],
                        start=True, stop=True))
                    P.barrier()
                    P.op("vector", lambda e: e.reduce_max(
                        C.AUX[0:1, 200:201], C.PS[4][0:1, :], axis=AX.X))
                    P.barrier()
                    P.op("vector", lambda e, h=h, cell0=cell0: e.tensor_max(
                        C.RW[0:1, cell0 + h:cell0 + h + 1],
                        C.RW[0:1, cell0 + h:cell0 + h + 1], C.AUX[0:1, 200:201]))
                    P.barrier()
    P.op("scalar", lambda e: e.activation(C.RW[0:1, 32:64], C.RW[0:1, 0:32],
                                          AF.Sqrt))
    P.barrier()
    P.op("vector", lambda e: e.tensor_mul(C.RW[0:1, 64:80], C.RW[0:1, 32:48],
                                          C.RW[0:1, 48:64]))
    P.barrier()
    P.op("vector", lambda e, scale=scale: e.tensor_scalar_mul(
        C.RW[0:1, 64:80], C.RW[0:1, 64:80], -scale))
    P.barrier()
    P.dma("sync", C.ROWD[0:1, 0:16], C.RW[0:1, 64:80])
    P.barrier()
    P.dma("sync", C.AUX[:, 100:116], C.ROWD[0:1, 0:16].broadcast_to([128, 16]))
    P.barrier()


def emit_softmax_attn(C, q_d, k_d, v_d, y_d, Tk, kt_range, mask_of, scale=0.125):
    P = C.P
    nkt = Tk // 128
    for h in range(16):
        P.dma("sync", C.WB[0:64, 0:Tk], k_d[h * 64:(h + 1) * 64, :])
        P.dma("sync", C.YB[0:64, 0:2048], q_d[h * 64:(h + 1) * 64, :])
        P.op("gpsimd", lambda e, nkt=nkt: e.memset(C.ZB[:, 0:nkt * 65], 1.0))
        P.barrier()
        P.dma3("sync", sb3(C.ZB[:, 0:nkt * 65], nkt, 65)[:, :, 0:64],
               v_d[:, h * 64:(h + 1) * 64].rearrange("(kt p) d -> p kt d", p=128),
               max_chunks=8)
        P.barrier()
        for qg in range(4):
            pout = C.PS[2][0:65, :]
            kt0, kt1 = kt_range(qg)
            for kt in range(kt0, kt1):
                ps = C.PS[kt % 2]
                P.op("tensor", lambda e, kt=kt, ps=ps, qg=qg: e.matmul(
                    ps, C.WB[0:64, kt * 128:(kt + 1) * 128],
                    C.YB[0:64, qg * 512:(qg + 1) * 512], start=True, stop=True))
                P.barrier()
                msk = mask_of(kt, qg)
                if msk is not None:
                    P.op("vector", lambda e, ps=ps, msk=msk: e.tensor_add(
                        ps, ps, msk))
                    P.barrier()
                eb = C.YB[:, 8192 + (kt % 2) * 512: 8192 + (kt % 2) * 512 + 512]
                P.op("scalar", lambda e, ps=ps, eb=eb, h=h: e.activation(
                    eb, ps, AF.Exp, scale=scale, bias=C.AUX[:, 100 + h:101 + h]))
                P.barrier()
                P.op("tensor", lambda e, kt=kt, eb=eb, pout=pout, kt0=kt0,
                     kt1=kt1: e.matmul(
                    pout, C.ZB[:, kt * 65:(kt + 1) * 65], eb,
                    start=(kt == kt0), stop=(kt == kt1 - 1)))
                P.barrier()
            P.op("vector", lambda e, pout=pout: e.reciprocal(
                C.RW[0:1, 512:1024], pout[64:65, :]))
            P.barrier()
            P.dma("sync", C.ROWD[0:1, 0:512], C.RW[0:1, 512:1024])
            P.barrier()
            P.dma("sync", C.AUX2[0:64, 0:512],
                  C.ROWD[0:1, 0:512].broadcast_to([64, 512]))
            P.barrier()
            P.op("vector", lambda e, pout=pout: e.tensor_mul(
                C.YB[0:64, 9216:9728], pout[0:64, :], C.AUX2[0:64, 0:512]))
            P.barrier()
            P.dma("sync", y_d[h * 64:(h + 1) * 64, qg * 512:(qg + 1) * 512],
                  C.YB[0:64, 9216:9728])
            P.barrier()


def build_l3a():
    nc = bass.Bass("TRN2", target_bir_lowering=False)
    xf_d = nc.dram_tensor("xf", [D, 4096], F32, kind="ExternalInput").ap()
    wqkv_d = nc.dram_tensor("wqkv", [D, 3072], F32, kind="ExternalInput").ap()
    qg_d = nc.dram_tensor("qg", [64], F32, kind="ExternalInput").ap()
    qb_d = nc.dram_tensor("qb", [64], F32, kind="ExternalInput").ap()
    kg_d = nc.dram_tensor("kg", [64], F32, kind="ExternalInput").ap()
    kb_d = nc.dram_tensor("kb", [64], F32, kind="ExternalInput").ap()
    wp_d = nc.dram_tensor("wp", [D, D], F32, kind="ExternalInput").ap()
    bp_d = nc.dram_tensor("bp", [D], F32, kind="ExternalInput").ap()
    xo_d = nc.dram_tensor("xo", [D, TOK], F32, kind="ExternalOutput").ap()
    lnxf_d = nc.dram_tensor("lnxf", [D, 4096], BF16, kind="Internal").ap()
    q_d = nc.dram_tensor("q_d", [D, 2048], BF16, kind="Internal").ap()
    k_d = nc.dram_tensor("k_d", [D, 4096], BF16, kind="Internal").ap()
    v_d = nc.dram_tensor("v_d", [4096, D], BF16, kind="Internal").ap()
    y_d = nc.dram_tensor("y_d", [D, 2048], BF16, kind="Internal").ap()

    C = Ctx(nc)
    P = C.P
    C.emit_identities()
    emit_ln(C, xf_d, 4096, lnxf_d, keep_half=0)

    wq_b = nc.dram_tensor("wq_b", [D, 3072], BF16, kind="Internal").ap()
    for j in range(6):
        P.dma3("sync", sb3(C.WF[:, 0:4096], 8, 512),
               wqkv_d[:, j * 512:(j + 1) * 512].rearrange("(c p) o -> p c o", p=128))
        P.barrier()
        P.op("vector", lambda e: e.tensor_copy(C.ZB[:, 0:4096], C.WF[:, 0:4096]))
        P.barrier()
        P.dma3("sync", wq_b[:, j * 512:(j + 1) * 512]
               .rearrange("(c p) o -> p c o", p=128), sb3(C.ZB[:, 0:4096], 8, 512))
        P.barrier()

    for (off, g_, b_) in ((0, qg_d, qb_d), (1024, kg_d, kb_d)):
        P.dma("sync", sb3(C.SM[:, off:off + 512], 8, 64),
              g_.unsqueeze(0).unsqueeze(0).broadcast_to([128, 8, 64]))
        P.dma("sync", sb3(C.SM[:, off + 512:off + 1024], 8, 64),
              b_.unsqueeze(0).unsqueeze(0).broadcast_to([128, 8, 64]))
    P.barrier()

    emit_qkv_tokmajor(C, lnxf_d, wq_b, 0, 1024, 2048,
                      lambda wg, tt, ps: emit_headln_evac(C, q_d, 0, wg, tt, ps),
                      0)
    emit_qkv_tokmajor(C, lnxf_d, wq_b, 1024, 1024, 4096,
                      lambda wg, tt, ps: emit_headln_evac(C, k_d, 1024, wg, tt, ps),
                      0)

    def v_evac(wg, tt, ps):
        P.op("vector", lambda e, ps=ps: e.tensor_copy(C.YB[:, 2048:2560], ps))
        P.barrier()
        P.dma("sync", v_d[tt * 128:(tt + 1) * 128, wg * 512:(wg + 1) * 512],
              C.YB[:, 2048:2560])
    emit_qkv_tokmajor(C, lnxf_d, wq_b, 2048, 1024, 4096, v_evac, 0)
    P.barrier()

    emit_norm_bound(C, q_d, k_d, 4096, 0.125)
    emit_softmax_attn(C, q_d, k_d, v_d, y_d, 4096,
                      lambda qg: (0, 32), lambda kt, qg: None)

    P.dma("sync", C.AUX[:, 32:40], bp_d.rearrange("(c p) -> p c", p=128),
          allow_slow_non_contiguous=True)
    P.barrier()

    def stage_tg(tg):
        P.dma3("sync", sb3(C.ZB[:, 0:8 * 512], 8, 512),
               rearr_fm(y_d, 0, 8, tg * 512, (tg + 1) * 512))
        P.barrier()

    def src_of(ic, tg):
        return C.ZB[:, ic * 512:(ic + 1) * 512]

    def evac(ofc, tg, ps):
        xsl = C.XB[:, ofc * 2048 + tg * 512: ofc * 2048 + (tg + 1) * 512]
        P.op("vector", lambda e, ps=ps, xsl=xsl, ofc=ofc: e.scalar_tensor_tensor(
            C.AUX2, ps, C.AUX[:, 32 + ofc:33 + ofc], xsl, ALU.add, ALU.add))
        P.barrier()
        P.dma("sync", xo_d[ofc * 128:(ofc + 1) * 128, tg * 512:(tg + 1) * 512],
              C.AUX2)

    emit_proj(C, wp_d, D, D, TOK, src_of, evac, stage_tg=stage_tg, wdtype="f32")
    P.barrier()
    P.final_wait()
    P.emit()
    return nc



def build_l0a(half):
    nc = bass.Bass("TRN2", target_bir_lowering=False)
    xf_d = nc.dram_tensor("xf", [D, 4096], F32, kind="ExternalInput").ap()
    wq_d = nc.dram_tensor("wq", [D, D], F32, kind="ExternalInput").ap()
    wkv_d = nc.dram_tensor("wkv", [D, 256], F32, kind="ExternalInput").ap()
    wku_d = nc.dram_tensor("wku", [256, D], F32, kind="ExternalInput").ap()
    wvu_d = nc.dram_tensor("wvu", [256, D], F32, kind="ExternalInput").ap()
    wo_d = nc.dram_tensor("wo", [D, D], F32, kind="ExternalInput").ap()
    xo_d = nc.dram_tensor("xo", [D, TOK], F32, kind="ExternalOutput").ap()
    Tk = 2048 if half == 0 else 4096
    lnxf_d = nc.dram_tensor("lnxf", [D, 4096], BF16, kind="Internal").ap()
    q_d = nc.dram_tensor("q_d", [D, 2048], BF16, kind="Internal").ap()
    lat_d = nc.dram_tensor("lat_d", [256, Tk], BF16, kind="Internal").ap()
    k_d = nc.dram_tensor("k_d", [D, Tk], BF16, kind="Internal").ap()
    v_d = nc.dram_tensor("v_d", [Tk, D], BF16, kind="Internal").ap()
    y_d = nc.dram_tensor("y_d", [D, 2048], BF16, kind="Internal").ap()
    wvu_b = nc.dram_tensor("wvu_b", [256, D], BF16, kind="Internal").ap()

    C = Ctx(nc)
    P = C.P
    C.emit_identities()
    emit_ln(C, xf_d, 4096, lnxf_d, keep_half=half)

    P.dma3("sync", sb3(C.WF[:, 0:2048], 2, 1024),
           wvu_d.rearrange("(c p) o -> p c o", p=128))
    P.barrier()
    P.op("vector", lambda e: e.tensor_copy(C.ZB[:, 0:2048], C.WF[:, 0:2048]))
    P.barrier()
    P.dma3("sync", wvu_b.rearrange("(c p) o -> p c o", p=128),
           sb3(C.ZB[:, 0:2048], 2, 1024))
    P.barrier()

    def bf_evac(dst_d):
        def evac(ofc, tg, ps):
            P.op("vector", lambda e, ps=ps: e.tensor_copy(C.YB[:, 0:512], ps))
            P.barrier()
            P.dma("sync", dst_d[ofc * 128:(ofc + 1) * 128,
                                tg * 512:(tg + 1) * 512], C.YB[:, 0:512])
        return evac

    def stage_lnxf(t0):
        def stage_tg(tg):
            P.dma3("sync", sb3(C.ZB[:, 8192:8192 + 8 * 512], 8, 512),
                   rearr_fm(lnxf_d, 0, 8, t0 + tg * 512, t0 + (tg + 1) * 512))
            P.barrier()
        return stage_tg

    def src_zb(ic, tg):
        return C.ZB[:, 8192 + ic * 512:8192 + (ic + 1) * 512]

    emit_proj(C, wq_d, D, D, 2048, src_zb, bf_evac(q_d),
              stage_tg=stage_lnxf(half * 2048), wdtype="f32")
    emit_proj(C, wkv_d, D, 256, Tk, src_zb, bf_evac(lat_d),
              stage_tg=stage_lnxf(0), wdtype="f32")

    def stage_lat(tg):
        P.dma3("sync", sb3(C.ZB[:, 8192:8192 + 2 * 512], 2, 512),
               rearr_fm(lat_d, 0, 2, tg * 512, (tg + 1) * 512))
        P.barrier()

    def src_lat(ic, tg):
        return C.ZB[:, 8192 + ic * 512:8192 + (ic + 1) * 512]

    emit_proj(C, wku_d, 256, D, Tk, src_lat, bf_evac(k_d),
              stage_tg=stage_lat, wdtype="f32")

    def v_evac(wg, tt, ps):
        P.op("vector", lambda e, ps=ps: e.tensor_copy(C.YB[:, 2048:2560], ps))
        P.barrier()
        P.dma("sync", v_d[tt * 128:(tt + 1) * 128, wg * 512:(wg + 1) * 512],
              C.YB[:, 2048:2560])
    emit_qkv_tokmajor(C, lat_d, wvu_b, 0, 1024, Tk, v_evac, 0, nic=2)
    P.barrier()

    emit_norm_bound(C, q_d, k_d, Tk, 0.125)

    P.op("gpsimd", lambda e: e.memset(C.SM[:, 0:2048], 0.0))
    P.barrier()
    for j in range(4):
        P.op("gpsimd", lambda e, j=j: e.affine_select(
            out=C.SM[:, j * 512:(j + 1) * 512],
            in_=C.SM[:, j * 512:(j + 1) * 512],
            compare_op=ALU.is_ge, fill=-1e30, base=-j * 128,
            pattern=[[1, 512]], channel_multiplier=-1))
    P.barrier()

    Q0s = [half * 2048 + qg * 512 for qg in range(4)]

    def kt_range(qg):
        return (0, (Q0s[qg] + 512) // 128)

    def mask_of(kt, qg):
        j = kt - Q0s[qg] // 128
        if 0 <= j < 4:
            return C.SM[:, j * 512:(j + 1) * 512]
        return None

    emit_softmax_attn(C, q_d, k_d, v_d, y_d, Tk, kt_range, mask_of)

    def stage_y(tg):
        P.dma3("sync", sb3(C.ZB[:, 0:8 * 512], 8, 512),
               rearr_fm(y_d, 0, 8, tg * 512, (tg + 1) * 512))
        P.barrier()

    def src_y(ic, tg):
        return C.ZB[:, ic * 512:(ic + 1) * 512]

    def evac(ofc, tg, ps):
        xsl = C.XB[:, ofc * 2048 + tg * 512: ofc * 2048 + (tg + 1) * 512]
        P.op("vector", lambda e, ps=ps, xsl=xsl: e.scalar_tensor_tensor(
            C.AUX2, ps, 1.0, xsl, ALU.mult, ALU.add))
        P.barrier()
        P.dma("sync", xo_d[ofc * 128:(ofc + 1) * 128, tg * 512:(tg + 1) * 512],
              C.AUX2)

    emit_proj(C, wo_d, D, D, TOK, src_y, evac, stage_tg=stage_y, wdtype="f32")
    P.barrier()
    P.final_wait()
    P.emit()
    return nc



def emit_elu_evac(C, P, ps, out_bf, n):
    P.op("scalar", lambda e, ps=ps: e.activation(C.AUX[:, 0:n], ps, AF.Exp))
    P.barrier()
    P.op("vector", lambda e, ps=ps: e.tensor_scalar_add(C.AUX2[:, 0:n], ps, 1.0))
    P.barrier()
    mi = C.WF[:, 2048:2048 + n].bitcast(I32)
    P.op("vector", lambda e, ps=ps, mi=mi: e.tensor_single_scalar(
        mi, ps, 0.0, ALU.is_gt))
    P.barrier()
    P.op("vector", lambda e, mi=mi, out_bf=out_bf: e.select(
        out_bf, mi, C.AUX2[:, 0:n], C.AUX[:, 0:n], add_drain=True))
    P.barrier()


def build_l2a(half, stage=99):
    import os
    stage = int(os.environ.get("L2A_STAGE", stage))
    nc = bass.Bass("TRN2", target_bir_lowering=False)
    xf_d = nc.dram_tensor("xf", [D, 4096], F32, kind="ExternalInput").ap()
    wq_d = nc.dram_tensor("wq", [D, 256], F32, kind="ExternalInput").ap()
    wk_d = nc.dram_tensor("wk", [D, 256], F32, kind="ExternalInput").ap()
    wv_d = nc.dram_tensor("wv", [D, 256], F32, kind="ExternalInput").ap()
    wo_d = nc.dram_tensor("wo", [256, D], F32, kind="ExternalInput").ap()
    gate_d = nc.dram_tensor("gate", [1], F32, kind="ExternalInput").ap()
    mem_d = nc.dram_tensor("mem", [256, 256], F32, kind="ExternalInput").ap()
    mn_d = nc.dram_tensor("mn", [256], F32, kind="ExternalInput").ap()
    xo_d = nc.dram_tensor("xo", [D, TOK], F32, kind="ExternalOutput").ap()
    Tk = 2048 if half == 0 else 4096
    T0 = half * 2048
    lnxf_d = nc.dram_tensor("lnxf", [D, 4096], BF16, kind="Internal").ap()
    sq_d = nc.dram_tensor("sq_d", [256, 2048], BF16, kind="Internal").ap()
    sk_d = nc.dram_tensor("sk_d", [256, Tk], BF16, kind="Internal").ap()
    skt_d = nc.dram_tensor("skt_d", [Tk, 256], BF16, kind="Internal").ap()
    v_d = nc.dram_tensor("v_d", [Tk, 256], BF16, kind="Internal").ap()
    memo_d = nc.dram_tensor("memo_d", [256, 2048], F32, kind="Internal").ap()
    loc_d = nc.dram_tensor("loc_d", [256, 2048], F32, kind="Internal").ap()
    comb_d = nc.dram_tensor("comb_d", [256, 2048], BF16, kind="Internal").ap()
    wk_b = nc.dram_tensor("wk_b", [D, 256], BF16, kind="Internal").ap()
    wv_b = nc.dram_tensor("wv_b", [D, 256], BF16, kind="Internal").ap()

    C = Ctx(nc)
    P = C.P
    C.emit_identities()
    emit_ln(C, xf_d, 4096, lnxf_d, keep_half=half)

    for (wsrc, wdst) in ((wk_d, wk_b), (wv_d, wv_b)):
        P.dma3("sync", sb3(C.WF[:, 0:2048], 8, 256),
               wsrc.rearrange("(c p) o -> p c o", p=128))
        P.barrier()
        P.op("vector", lambda e: e.tensor_copy(C.ZB[:, 0:2048], C.WF[:, 0:2048]))
        P.barrier()
        P.dma3("sync", wdst.rearrange("(c p) o -> p c o", p=128),
               sb3(C.ZB[:, 0:2048], 8, 256))
        P.barrier()

    def stage_lnxf(t0):
        def stage_tg(tg):
            P.dma3("sync", sb3(C.ZB[:, 8192:8192 + 8 * 512], 8, 512),
                   rearr_fm(lnxf_d, 0, 8, t0 + tg * 512, t0 + (tg + 1) * 512))
            P.barrier()
        return stage_tg

    def src_zb(ic, tg):
        return C.ZB[:, 8192 + ic * 512:8192 + (ic + 1) * 512]

    def sq_evac(ofc, tg, ps):
        emit_elu_evac(C, P, ps, C.YB[:, 0:512], 512)
        P.dma("sync", sq_d[ofc * 128:(ofc + 1) * 128, tg * 512:(tg + 1) * 512],
              C.YB[:, 0:512])
    emit_proj(C, wq_d, D, 256, 2048, src_zb, sq_evac,
              stage_tg=stage_lnxf(T0), wdtype="f32")

    def sk_evac(ofc, tg, ps):
        emit_elu_evac(C, P, ps, C.YB[:, 0:512], 512)
        P.dma("sync", sk_d[ofc * 128:(ofc + 1) * 128, tg * 512:(tg + 1) * 512],
              C.YB[:, 0:512])
    emit_proj(C, wk_d, D, 256, Tk, src_zb, sk_evac,
              stage_tg=stage_lnxf(0), wdtype="f32")

    def skt_evac(wg, tt, ps):
        emit_elu_evac(C, P, ps, C.YB[:, 2048:2304], 256)
        P.dma("sync", skt_d[tt * 128:(tt + 1) * 128, :], C.YB[:, 2048:2304])
    emit_qkv_tokmajor(C, lnxf_d, wk_b, 0, 256, Tk, skt_evac, 0)

    def v_evac(wg, tt, ps):
        P.op("vector", lambda e, ps=ps: e.tensor_copy(C.YB[:, 2048:2304], ps))
        P.barrier()
        P.dma("sync", v_d[tt * 128:(tt + 1) * 128, :], C.YB[:, 2048:2304])
    emit_qkv_tokmajor(C, lnxf_d, wv_b, 0, 256, Tk, v_evac, 0)
    P.barrier()
    def _early():
        for c in range(8):
            P.dma("sync", xo_d[c * 128:(c + 1) * 128, :],
                  C.XB[:, c * 2048:(c + 1) * 2048])
        P.barrier()
        P.final_wait()
        P.emit()
        return nc
    if stage <= 1:
        return _early()

    P.dma3("sync", sb3(C.WF[:, 0:512], 2, 256),
           mem_d.rearrange("(c p) o -> p c o", p=128))
    P.barrier()
    P.op("vector", lambda e: e.tensor_copy(C.WB[:, 0:512], C.WF[:, 0:512]))
    P.barrier()
    P.dma("sync", C.AUX[:, 0:2], mn_d.rearrange("(c p) -> p c", p=128),
          allow_slow_non_contiguous=True)
    P.dma("sync", C.RW[0:1, 1024:1280], mn_d.unsqueeze(0))
    P.barrier()
    P.op("vector", lambda e: e.tensor_copy(C.WB[:, 512:514], C.AUX[:, 0:2]))
    P.op("vector", lambda e: e.reduce_sum(C.RW[0:1, 1280:1281],
                                          C.RW[0:1, 1024:1280], axis=AX.X))
    P.barrier()
    P.op("vector", lambda e: e.tensor_single_scalar(
        C.RW[0:1, 1281:1282], C.RW[0:1, 1280:1281], 1e-6, ALU.is_ge))
    P.barrier()
    P.dma("sync", C.ROWD[0:1, 600:601], C.RW[0:1, 1281:1282])
    P.barrier()
    P.dma("sync", C.CONST[:, 2:3], C.ROWD[0:1, 600:601].broadcast_to([128, 1]))
    P.barrier()
    for tg in range(4):
        P.dma3("sync", sb3(C.ZB[:, 0:1024], 2, 512),
               rearr_fm(sq_d, 0, 2, tg * 512, (tg + 1) * 512))
        P.barrier()
        for dc in range(2):
            P.op("tensor", lambda e, dc=dc: e.matmul(
                C.PS[4][0:1, :], C.WB[:, 512 + dc:513 + dc],
                C.ZB[:, dc * 512:(dc + 1) * 512],
                start=(dc == 0), stop=(dc == 1)))
        P.barrier()
        P.op("vector", lambda e: e.tensor_scalar_max(
            C.RW[0:1, 1536:2048], C.PS[4][0:1, :], 1e-6))
        P.barrier()
        P.op("vector", lambda e: e.reciprocal(C.RW[0:1, 1536:2048],
                                              C.RW[0:1, 1536:2048]))
        P.barrier()
        P.dma("sync", C.ROWD[0:1, 0:512], C.RW[0:1, 1536:2048])
        P.barrier()
        P.dma("sync", C.AUX2, C.ROWD[0:1, 0:512].broadcast_to([128, 512]))
        P.barrier()
        for ofc in range(2):
            ps = C.PS[ofc]
            for dc in range(2):
                P.op("tensor", lambda e, dc=dc, ofc=ofc, ps=ps: e.matmul(
                    ps, C.WB[:, dc * 256 + ofc * 128: dc * 256 + (ofc + 1) * 128],
                    C.ZB[:, dc * 512:(dc + 1) * 512],
                    start=(dc == 0), stop=(dc == 1)))
            P.barrier()
            P.op("vector", lambda e, ps=ps: e.scalar_tensor_tensor(
                C.AUX, ps, C.CONST[:, 2:3], C.AUX2, ALU.mult, ALU.mult))
            P.barrier()
            P.dma("sync", memo_d[ofc * 128:(ofc + 1) * 128,
                                 tg * 512:(tg + 1) * 512], C.AUX)
    P.barrier()
    if stage <= 2:
        return _early()

    P.op("gpsimd", lambda e: e.memset(C.MSK, 1.0))
    P.barrier()
    P.op("gpsimd", lambda e: e.affine_select(
        out=C.MSK, in_=C.MSK, compare_op=ALU.is_ge, fill=0.0, base=0,
        pattern=[[1, 128]], channel_multiplier=-1))
    P.barrier()
    P.op("gpsimd", lambda e: e.memset(C.WF[:, 2048:2562], 0.0))
    P.op("gpsimd", lambda e: e.memset(C.WB[:, 1024:1538], 0.0))
    P.barrier()

    def stage_kv(cc_glob):
        P.dma("sync", C.ZB[:, 1536:1792],
              skt_d[cc_glob * 128:(cc_glob + 1) * 128, :])
        P.dma("sync", C.ZB[:, 1792:2048],
              v_d[cc_glob * 128:(cc_glob + 1) * 128, :])
        P.barrier()

    def carry_update(cc_glob, staged=False):
        if not staged:
            stage_kv(cc_glob)
        for dc in range(2):
            P.op("tensor", lambda e, dc=dc: e.matmul(
                C.PS[4][:, 0:256], C.ZB[:, 1536 + dc * 128:1536 + (dc + 1) * 128],
                C.ZB[:, 1792:2048], start=True, stop=True))
            P.barrier()
            P.op("vector", lambda e, dc=dc: e.tensor_add(
                C.WF[:, 2048 + dc * 256:2048 + (dc + 1) * 256],
                C.WF[:, 2048 + dc * 256:2048 + (dc + 1) * 256],
                C.PS[4][:, 0:256]))
            P.barrier()
            P.op("tensor", lambda e, dc=dc: e.matmul(
                C.PS[5][:, 0:1], C.ZB[:, 1536 + dc * 128:1536 + (dc + 1) * 128],
                C.CONSTB[:, 1:2], start=True, stop=True))
            P.barrier()
            P.op("vector", lambda e, dc=dc: e.tensor_add(
                C.WF[:, 2560 + dc:2561 + dc], C.WF[:, 2560 + dc:2561 + dc],
                C.PS[5][:, 0:1]))
            P.barrier()
        P.op("vector", lambda e: e.tensor_copy(C.WB[:, 1024:1536],
                                               C.WF[:, 2048:2560]))
        P.op("vector", lambda e: e.tensor_copy(C.WB[:, 1536:1538],
                                               C.WF[:, 2560:2562]))
        P.barrier()

    if half == 1:
        for cc in range(16):
            carry_update(cc)

    for cc in range(16):
        cg = T0 // 128 + cc
        P.dma("sync", C.ZB[:, 1024:1152], sk_d[0:128, cg * 128:(cg + 1) * 128])
        P.dma("sync", C.ZB[:, 1152:1280], sk_d[128:256, cg * 128:(cg + 1) * 128])
        P.dma("sync", C.ZB[:, 1280:1408], sq_d[0:128, cc * 128:(cc + 1) * 128])
        P.dma("sync", C.ZB[:, 1408:1536], sq_d[128:256, cc * 128:(cc + 1) * 128])
        P.barrier()
        stage_kv(cg)
        for dc in range(2):
            P.op("tensor", lambda e, dc=dc: e.matmul(
                C.PS[0][:, 0:128], C.ZB[:, 1024 + dc * 128:1024 + (dc + 1) * 128],
                C.ZB[:, 1280 + dc * 128:1280 + (dc + 1) * 128],
                start=(dc == 0), stop=(dc == 1)))
        P.barrier()
        P.op("vector", lambda e: e.tensor_mul(C.YB[:, 0:128], C.PS[0][:, 0:128],
                                              C.MSK))
        P.barrier()
        for vc in range(2):
            ps = C.PS[1 + vc]
            P.op("tensor", lambda e, vc=vc, ps=ps: e.matmul(
                ps[:, 0:128], C.ZB[:, 1792 + vc * 128:1792 + (vc + 1) * 128],
                C.YB[:, 0:128], start=True, stop=False))
            for dc in range(2):
                P.op("tensor", lambda e, vc=vc, dc=dc, ps=ps: e.matmul(
                    ps[:, 0:128],
                    C.WB[:, 1024 + dc * 256 + vc * 128:
                          1024 + dc * 256 + (vc + 1) * 128],
                    C.ZB[:, 1280 + dc * 128:1280 + (dc + 1) * 128],
                    start=False, stop=(dc == 1)))
        P.barrier()
        P.op("tensor", lambda e: e.matmul(
            C.PS[3][0:1, 0:128], C.CONSTB[:, 1:2], C.YB[:, 0:128],
            start=True, stop=False))
        for dc in range(2):
            P.op("tensor", lambda e, dc=dc: e.matmul(
                C.PS[3][0:1, 0:128], C.WB[:, 1536 + dc:1537 + dc],
                C.ZB[:, 1280 + dc * 128:1280 + (dc + 1) * 128],
                start=False, stop=(dc == 1)))
        P.barrier()
        P.op("vector", lambda e: e.tensor_scalar_max(
            C.RW[0:1, 1536:1664], C.PS[3][0:1, 0:128], 1e-6))
        P.barrier()
        P.op("vector", lambda e: e.reciprocal(C.RW[0:1, 1536:1664],
                                              C.RW[0:1, 1536:1664]))
        P.barrier()
        P.dma("sync", C.ROWD[0:1, 0:128], C.RW[0:1, 1536:1664])
        P.barrier()
        P.dma("sync", C.AUX2[:, 0:128],
              C.ROWD[0:1, 0:128].broadcast_to([128, 128]))
        P.barrier()
        for vc in range(2):
            P.op("vector", lambda e, vc=vc: e.tensor_mul(
                C.AUX[:, 0:128], C.PS[1 + vc][:, 0:128], C.AUX2[:, 0:128]))
            P.barrier()
            P.dma("sync", loc_d[vc * 128:(vc + 1) * 128,
                                cc * 128:(cc + 1) * 128], C.AUX[:, 0:128])
            P.barrier()
        carry_update(cg, staged=True)
    P.barrier()
    if stage <= 3:
        return _early()

    P.dma("sync", C.RW[0:1, 1100:1101], gate_d.unsqueeze(0))
    P.barrier()
    P.op("scalar", lambda e: e.activation(C.RW[0:1, 1100:1101],
                                          C.RW[0:1, 1100:1101], AF.Sigmoid))
    P.barrier()
    P.op("vector", lambda e: e.tensor_scalar(
        C.RW[0:1, 1101:1102], C.RW[0:1, 1100:1101], -1.0, 1.0,
        ALU.mult, ALU.add))
    P.barrier()
    P.dma("sync", C.ROWD[0:1, 601:603], C.RW[0:1, 1100:1102])
    P.barrier()
    P.dma("sync", C.CONST[:, 3:4], C.ROWD[0:1, 601:602].broadcast_to([128, 1]))
    P.dma("sync", C.CONST[:, 4:5], C.ROWD[0:1, 602:603].broadcast_to([128, 1]))
    P.barrier()
    for vc in range(2):
        for tg in range(4):
            P.dma("sync", C.AUX, memo_d[vc * 128:(vc + 1) * 128,
                                        tg * 512:(tg + 1) * 512])
            P.dma("sync", C.AUX2, loc_d[vc * 128:(vc + 1) * 128,
                                        tg * 512:(tg + 1) * 512])
            P.barrier()
            P.op("vector", lambda e: e.tensor_scalar_mul(
                C.AUX2, C.AUX2, C.CONST[:, 4:5]))
            P.barrier()
            P.op("vector", lambda e: e.scalar_tensor_tensor(
                C.YB[:, 0:512], C.AUX, C.CONST[:, 3:4], C.AUX2,
                ALU.mult, ALU.add))
            P.barrier()
            P.dma("sync", comb_d[vc * 128:(vc + 1) * 128,
                                 tg * 512:(tg + 1) * 512], C.YB[:, 0:512])
            P.barrier()
    P.barrier()
    if stage <= 4:
        return _early()

    def stage_comb(tg):
        P.dma3("sync", sb3(C.ZB[:, 0:1024], 2, 512),
               rearr_fm(comb_d, 0, 2, tg * 512, (tg + 1) * 512))
        P.barrier()

    def src_comb(ic, tg):
        return C.ZB[:, ic * 512:(ic + 1) * 512]

    def evac(ofc, tg, ps):
        xsl = C.XB[:, ofc * 2048 + tg * 512: ofc * 2048 + (tg + 1) * 512]
        P.op("vector", lambda e, ps=ps, xsl=xsl: e.scalar_tensor_tensor(
            C.AUX2, ps, 1.0, xsl, ALU.mult, ALU.add))
        P.barrier()
        P.dma("sync", xo_d[ofc * 128:(ofc + 1) * 128, tg * 512:(tg + 1) * 512],
              C.AUX2)

    emit_proj(C, wo_d, 256, D, TOK, src_comb, evac, stage_tg=stage_comb,
              wdtype="f32")
    P.barrier()
    P.final_wait()
    P.emit()
    return nc




def build_l1a(half):
    nc = bass.Bass("TRN2", target_bir_lowering=False)
    xf_d = nc.dram_tensor("xf", [D, 4096], F32, kind="ExternalInput").ap()
    wq_d = nc.dram_tensor("wq", [D, D], F32, kind="ExternalInput").ap()
    wk_d = nc.dram_tensor("wk", [D, D], F32, kind="ExternalInput").ap()
    wv_d = nc.dram_tensor("wv", [D, D], F32, kind="ExternalInput").ap()
    wo_d = nc.dram_tensor("wo", [D, D], F32, kind="ExternalInput").ap()
    omt_d = nc.dram_tensor("omt", [64, 128], F32, kind="ExternalInput").ap()
    w1_d = nc.dram_tensor("w1", [D, 4096], F32, kind="ExternalInput").ap()
    b1_d = nc.dram_tensor("b1", [4096], F32, kind="ExternalInput").ap()
    w2_d = nc.dram_tensor("w2", [4096, D], F32, kind="ExternalInput").ap()
    b2_d = nc.dram_tensor("b2", [D], F32, kind="ExternalInput").ap()
    gam_d = nc.dram_tensor("gam", [D], F32, kind="ExternalInput").ap()
    bet_d = nc.dram_tensor("bet", [D], F32, kind="ExternalInput").ap()
    xo_d = nc.dram_tensor("xo", [D, TOK], F32, kind="ExternalOutput").ap()
    Tk = 2048 if half == 0 else 4096
    T0 = half * 2048
    nck = Tk // 128
    lnxf_d = nc.dram_tensor("lnxf", [D, 4096], BF16, kind="Internal").ap()
    q_d = nc.dram_tensor("q_d", [D, 2048], BF16, kind="Internal").ap()
    k_d = nc.dram_tensor("k_d", [D, Tk], BF16, kind="Internal").ap()
    v_d = nc.dram_tensor("v_d", [Tk, D], BF16, kind="Internal").ap()
    att_d = nc.dram_tensor("att_d", [D, 2048], BF16, kind="Internal").ap()
    u_d = nc.dram_tensor("u_d", [D, TOK], F32, kind="Internal").ap()
    h2_d = nc.dram_tensor("h2_d", [4096, TOK], BF16, kind="Internal").ap()
    wv_b = nc.dram_tensor("wv_b", [D, D], BF16, kind="Internal").ap()
    omt_b = nc.dram_tensor("omt_b", [64, 128], BF16, kind="Internal").ap()

    C = Ctx(nc)
    P = C.P
    C.emit_identities()
    emit_ln(C, xf_d, 4096, lnxf_d, keep_half=half)

    for j in range(2):
        P.dma3("sync", sb3(C.WF[:, 0:4096], 8, 512),
               wv_d[:, j * 512:(j + 1) * 512].rearrange("(c p) o -> p c o", p=128))
        P.barrier()
        P.op("vector", lambda e: e.tensor_copy(C.ZB[:, 0:4096], C.WF[:, 0:4096]))
        P.barrier()
        P.dma3("sync", wv_b[:, j * 512:(j + 1) * 512]
               .rearrange("(c p) o -> p c o", p=128), sb3(C.ZB[:, 0:4096], 8, 512))
        P.barrier()
    P.dma("sync", C.WF[0:64, 0:128], omt_d)
    P.barrier()
    P.op("vector", lambda e: e.tensor_copy(C.ZB[0:64, 0:128], C.WF[0:64, 0:128]))
    P.barrier()
    P.dma("sync", omt_b, C.ZB[0:64, 0:128])
    P.barrier()

    def stage_lnxf(t0):
        def stage_tg(tg):
            P.dma3("sync", sb3(C.ZB[:, 8192:8192 + 8 * 512], 8, 512),
                   rearr_fm(lnxf_d, 0, 8, t0 + tg * 512, t0 + (tg + 1) * 512))
            P.barrier()
        return stage_tg

    def src_zb(ic, tg):
        return C.ZB[:, 8192 + ic * 512:8192 + (ic + 1) * 512]

    SC = 0.35355339059327373

    def scaled_evac(dst_d):
        def evac(ofc, tg, ps):
            P.op("scalar", lambda e, ps=ps: e.mul(C.YB[:, 0:512], ps, SC))
            P.barrier()
            P.dma("sync", dst_d[ofc * 128:(ofc + 1) * 128,
                                tg * 512:(tg + 1) * 512], C.YB[:, 0:512])
        return evac

    emit_proj(C, wq_d, D, D, 2048, src_zb, scaled_evac(q_d),
              stage_tg=stage_lnxf(T0), wdtype="f32")
    emit_proj(C, wk_d, D, D, Tk, src_zb, scaled_evac(k_d),
              stage_tg=stage_lnxf(0), wdtype="f32")

    def v_evac(wg, tt, ps):
        P.op("vector", lambda e, ps=ps: e.tensor_copy(C.YB[:, 2048:2560], ps))
        P.barrier()
        P.dma("sync", v_d[tt * 128:(tt + 1) * 128, wg * 512:(wg + 1) * 512],
              C.YB[:, 2048:2560])
    emit_qkv_tokmajor(C, lnxf_d, wv_b, 0, 1024, Tk, v_evac, 0)
    P.barrier()

    P.op("gpsimd", lambda e: e.memset(C.MSK, 1.0))
    P.barrier()
    P.op("gpsimd", lambda e: e.affine_select(
        out=C.MSK, in_=C.MSK, compare_op=ALU.is_ge, fill=0.0, base=0,
        pattern=[[1, 128]], channel_multiplier=-1))
    P.barrier()

    LOG_SQRT_M = 0.5 * 4.852030263919617

    def phi_pass(src_sb, ntiles, dst_tm_off, dst_fm_off):
        for tt in range(ntiles):
            P.op("tensor", lambda e, tt=tt: e.matmul(
                C.PS[0][:, 0:128], src_sb[:, tt * 128:(tt + 1) * 128],
                C.YB[0:64, 3072:3200], start=True, stop=True))
            P.barrier()
            P.op("vector", lambda e: e.reduce_max(C.AUX[:, 0:1],
                                                  C.PS[0][:, 0:128], axis=AX.X))
            P.barrier()
            P.op("vector", lambda e: e.tensor_scalar(
                C.AUX[:, 1:2], C.AUX[:, 0:1], LOG_SQRT_M, -1.0,
                ALU.add, ALU.mult))
            P.barrier()
            P.op("scalar", lambda e: e.activation(
                C.WF[:, 0:128], C.PS[0][:, 0:128], AF.Exp, bias=C.AUX[:, 1:2]))
            P.barrier()
            tm = C.ZB[:, dst_tm_off + tt * 128: dst_tm_off + (tt + 1) * 128] \
                if dst_tm_off is not None else C.YB[:, 8192:8320]
            P.op("vector", lambda e, tm=tm: e.tensor_scalar_add(
                tm, C.WF[:, 0:128], 1e-6))
            P.barrier()
            if dst_fm_off is not None:
                P.op("tensor", lambda e, tm=tm: e.transpose(
                    C.PT[:, 0:128], tm, C.IDB))
                P.barrier()
                fm = C.ZB[:, dst_fm_off + tt * 128: dst_fm_off + (tt + 1) * 128]
                P.op("vector", lambda e, fm=fm: e.tensor_copy(fm, C.PT[:, 0:128]))
                P.barrier()

    for h in range(16):
        P.dma("sync", C.YB[0:64, 0:2048], q_d[h * 64:(h + 1) * 64, :])
        P.dma("sync", C.YB[0:64, 4096:4096 + Tk], k_d[h * 64:(h + 1) * 64, :])
        P.dma("sync", C.YB[0:64, 3072:3200], omt_b)
        P.barrier()
        P.dma3("sync", sb3(C.ZB[:, 10240:10240 + nck * 64], nck, 64),
               v_d[:, h * 64:(h + 1) * 64].rearrange("(kt p) d -> p kt d", p=128),
               max_chunks=8)
        P.barrier()
        phi_pass(C.YB[0:64, 0:2048], 16, None, 0)
        phi_pass(C.YB[0:64, 4096:4096 + Tk], nck, 2048, 6144)
        P.op("gpsimd", lambda e: e.memset(C.WF[:, 2048:2113], 0.0))
        P.op("gpsimd", lambda e: e.memset(C.WB[:, 0:65], 0.0))
        P.barrier()

        def carry_update(ci):
            kt_t = C.ZB[:, 2048 + ci * 128:2048 + (ci + 1) * 128]
            vt = C.ZB[:, 10240 + ci * 64:10240 + (ci + 1) * 64]
            P.op("tensor", lambda e, kt_t=kt_t, vt=vt: e.matmul(
                C.PS[4][:, 0:64], kt_t, vt, start=True, stop=True))
            P.op("tensor", lambda e, kt_t=kt_t: e.matmul(
                C.PS[5][:, 0:1], kt_t, C.CONSTB[:, 1:2], start=True, stop=True))
            P.barrier()
            P.op("vector", lambda e: e.tensor_add(
                C.WF[:, 2048:2112], C.WF[:, 2048:2112], C.PS[4][:, 0:64]))
            P.op("vector", lambda e: e.tensor_add(
                C.WF[:, 2112:2113], C.WF[:, 2112:2113], C.PS[5][:, 0:1]))
            P.barrier()
            P.op("vector", lambda e: e.tensor_copy(C.WB[:, 0:65],
                                                   C.WF[:, 2048:2113]))
            P.barrier()

        if half == 1:
            for ci in range(16):
                carry_update(ci)
        for cc in range(16):
            ci = 16 * half + cc
            qf = C.ZB[:, cc * 128:(cc + 1) * 128]
            kf = C.ZB[:, 6144 + ci * 128:6144 + (ci + 1) * 128]
            vt = C.ZB[:, 10240 + ci * 64:10240 + (ci + 1) * 64]
            P.op("tensor", lambda e, kf=kf, qf=qf: e.matmul(
                C.PS[0][:, 0:128], kf, qf, start=True, stop=True))
            P.barrier()
            P.op("vector", lambda e: e.tensor_mul(
                C.YB[:, 8192:8320], C.PS[0][:, 0:128], C.MSK))
            P.barrier()
            P.op("tensor", lambda e, vt=vt: e.matmul(
                C.PS[1][0:64, 0:128], vt, C.YB[:, 8192:8320],
                start=True, stop=False))
            P.op("tensor", lambda e, qf=qf: e.matmul(
                C.PS[1][0:64, 0:128], C.WB[:, 0:64], qf,
                start=False, stop=True))
            P.op("tensor", lambda e: e.matmul(
                C.PS[3][0:1, 0:128], C.CONSTB[:, 1:2], C.YB[:, 8192:8320],
                start=True, stop=False))
            P.op("tensor", lambda e, qf=qf: e.matmul(
                C.PS[3][0:1, 0:128], C.WB[:, 64:65], qf,
                start=False, stop=True))
            P.barrier()
            P.op("vector", lambda e: e.tensor_scalar_add(
                C.RW[0:1, 1536:1664], C.PS[3][0:1, 0:128], 1e-6))
            P.barrier()
            P.op("vector", lambda e: e.reciprocal(C.RW[0:1, 1536:1664],
                                                  C.RW[0:1, 1536:1664]))
            P.barrier()
            P.dma("sync", C.ROWD[0:1, 0:128], C.RW[0:1, 1536:1664])
            P.barrier()
            P.dma("sync", C.AUX2[0:64, 0:128],
                  C.ROWD[0:1, 0:128].broadcast_to([64, 128]))
            P.barrier()
            P.op("vector", lambda e: e.tensor_mul(
                C.YB[0:64, 8320:8448], C.PS[1][0:64, 0:128], C.AUX2[0:64, 0:128]))
            P.barrier()
            P.dma("sync", att_d[h * 64:(h + 1) * 64, cc * 128:(cc + 1) * 128],
                  C.YB[0:64, 8320:8448])
            P.barrier()
            carry_update(ci)
    P.barrier()

    def stage_att(tg):
        P.dma3("sync", sb3(C.ZB[:, 0:8 * 512], 8, 512),
               rearr_fm(att_d, 0, 8, tg * 512, (tg + 1) * 512))
        P.barrier()

    def src_att(ic, tg):
        return C.ZB[:, ic * 512:(ic + 1) * 512]

    def u_evac(ofc, tg, ps):
        P.dma("sync", C.YB[:, 8192:8704],
              lnxf_d[ofc * 128:(ofc + 1) * 128,
                     T0 + tg * 512:T0 + (tg + 1) * 512])
        P.barrier()
        P.op("vector", lambda e, ps=ps: e.scalar_tensor_tensor(
            C.AUX, ps, 1.0, C.YB[:, 8192:8704], ALU.mult, ALU.add))
        P.barrier()
        P.dma("sync", u_d[ofc * 128:(ofc + 1) * 128, tg * 512:(tg + 1) * 512],
              C.AUX)

    emit_proj(C, wo_d, D, D, TOK, src_att, u_evac, stage_tg=stage_att,
              wdtype="f32")
    P.barrier()

    emit_ln(C, u_d, TOK, None, eps=1e6, gamma_d=gam_d, beta_d=bet_d)

    P.dma("sync", C.AUX[:, 64:96], b1_d.rearrange("(c p) -> p c", p=128),
          allow_slow_non_contiguous=True)
    P.dma("sync", C.AUX[:, 32:40], b2_d.rearrange("(c p) -> p c", p=128),
          allow_slow_non_contiguous=True)
    P.barrier()

    def src_yb(ic, tg):
        return C.YB[:, ic * 2048 + tg * 512: ic * 2048 + (tg + 1) * 512]

    def h_evac(ofc, tg, ps):
        P.op("scalar", lambda e, ps=ps, ofc=ofc: e.activation(
            C.ZB[:, 0:512], ps, AF.Relu, bias=C.AUX[:, 64 + ofc:65 + ofc]))
        P.barrier()
        P.dma("sync", h2_d[ofc * 128:(ofc + 1) * 128, tg * 512:(tg + 1) * 512],
              C.ZB[:, 0:512])

    emit_proj(C, w1_d, D, 4096, TOK, src_yb, h_evac, wdtype="f32")
    P.barrier()

    P.dma3("sync", sb3(C.XB[:, 0:8 * 2048], 8, 2048),
           rearr_fm(xf_d, 0, 8, T0, T0 + 2048))
    P.barrier()

    def stage_h2(tg):
        P.dma3("sync", sb3(C.ZB[:, 0:32 * 512], 32, 512),
               rearr_fm(h2_d, 0, 32, tg * 512, (tg + 1) * 512))
        P.barrier()

    def src_h2(ic, tg):
        return C.ZB[:, ic * 512:(ic + 1) * 512]

    def xo_evac(ofc, tg, ps):
        xsl = C.XB[:, ofc * 2048 + tg * 512: ofc * 2048 + (tg + 1) * 512]
        P.op("vector", lambda e, ps=ps, xsl=xsl, ofc=ofc: e.scalar_tensor_tensor(
            C.AUX2, ps, C.AUX[:, 32 + ofc:33 + ofc], xsl, ALU.add, ALU.add))
        P.barrier()
        P.dma("sync", xo_d[ofc * 128:(ofc + 1) * 128, tg * 512:(tg + 1) * 512],
              C.AUX2)

    emit_proj(C, w2_d, 4096, D, TOK, src_h2, xo_evac, stage_tg=stage_h2,
              wdtype="f32")
    P.barrier()
    P.final_wait()
    P.emit()
    return nc


def emit_routing(C, gw_d, nE, topk, comb_d, br_d=None):
    P = C.P
    P.dma("sync", sb3(C.WF[:, 0:8 * nE], 8, nE),
          gw_d.rearrange("(c p) o -> p c o", p=128))
    P.barrier()
    P.op("vector", lambda e: e.tensor_copy(C.WB[:, 0:8 * nE], C.WF[:, 0:8 * nE]))
    P.barrier()
    if br_d is not None:
        P.dma("sync", C.AUX[:, 48:48 + nE],
              br_d.unsqueeze(0).broadcast_to([128, nE]))
    P.op("gpsimd", lambda e: e.memset(C.AUX[:, 40:40 + nE], -1e30))
    P.barrier()
    PL = C.PS[2][:, 0:nE]
    PTR = C.PS[3]
    for tt in range(16):
        for ic in range(8):
            P.op("tensor", lambda e, ic=ic, tt=tt: e.matmul(
                PL, C.YB[:, ic * 2048 + tt * 128: ic * 2048 + (tt + 1) * 128],
                C.WB[:, ic * nE:(ic + 1) * nE],
                start=(ic == 0), stop=(ic == 7)))
        P.barrier()
        lg = C.AUX[:, 0:nE]
        if br_d is not None:
            P.op("vector", lambda e: e.tensor_add(lg, PL, C.AUX[:, 48:48 + nE]))
        else:
            P.op("vector", lambda e: e.tensor_copy(lg, PL))
        P.barrier()
        m1 = C.AUX[:, 8:9]
        P.op("vector", lambda e: e.reduce_max(m1, lg, axis=AX.X))
        P.barrier()
        is1f = C.AUX[:, 16:16 + nE]
        P.op("vector", lambda e: e.tensor_single_scalar(is1f, lg, m1, ALU.is_equal))
        P.barrier()
        comb = C.AUX2[:, 0:nE]
        if topk == 1:
            P.op("vector", lambda e: e.tensor_copy(comb, is1f))
            P.barrier()
        else:
            is1i = C.MSK.bitcast(I32)[:, 0:nE]
            P.op("vector", lambda e: e.tensor_single_scalar(
                is1i, lg, m1, ALU.is_equal))
            P.barrier()
            msked = C.AUX[:, 24:24 + nE]
            P.op("vector", lambda e: e.select(msked, is1i,
                                              C.AUX[:, 40:40 + nE], lg,
                                              add_drain=True))
            P.barrier()
            m2 = C.AUX[:, 9:10]
            P.op("vector", lambda e: e.reduce_max(m2, msked, axis=AX.X))
            P.barrier()
            is2f = C.AUX[:, 32:32 + nE]
            P.op("vector", lambda e: e.tensor_single_scalar(
                is2f, msked, m2, ALU.is_equal))
            P.barrier()
            dd = C.AUX[:, 10:11]
            P.op("vector", lambda e: e.tensor_sub(dd, m2, m1))
            P.barrier()
            P.op("scalar", lambda e: e.activation(dd, dd, AF.Exp))
            P.barrier()
            w1 = C.AUX[:, 11:12]
            P.op("vector", lambda e: e.tensor_scalar_add(w1, dd, 1.0))
            P.barrier()
            P.op("vector", lambda e: e.reciprocal(w1, w1))
            P.barrier()
            w2 = C.AUX[:, 12:13]
            P.op("vector", lambda e: e.tensor_mul(w2, dd, w1))
            P.barrier()
            P.op("vector", lambda e: e.tensor_scalar_mul(comb, is1f, w1))
            P.barrier()
            P.op("vector", lambda e: e.scalar_tensor_tensor(
                comb, is2f, w2, comb, ALU.mult, ALU.add))
            P.barrier()
        P.op("tensor", lambda e: e.transpose(
            PTR[0:nE, 0:128], comb, C.IDF))
        P.barrier()
        P.op("vector", lambda e: e.tensor_copy(C.AUX2[0:nE, 128:256],
                                               PTR[0:nE, 0:128]))
        P.barrier()
        P.dma("sync", comb_d[0:nE, tt * 128:(tt + 1) * 128],
              C.AUX2[0:nE, 128:256])
        P.barrier()


def emit_moe(C, cfg):
    P = C.P
    E, F = cfg["E"], cfg["F"]
    nfc = F // 128
    act = cfg["act"]
    P.dma("sync", C.WF[0:E, 0:1024], cfg["b2_d"])
    P.dma("sync", C.SM[0:E, 0:2048], cfg["comb_d"])
    P.barrier()
    P.op("vector", lambda e: e.tensor_copy(C.WB[0:E, 0:1024], C.WF[0:E, 0:1024]))
    P.op("vector", lambda e: e.tensor_copy(C.WB[0:E, 1024:3072], C.SM[0:E, 0:2048]))
    P.barrier()
    for ofc in range(8):
        for tg in range(4):
            ps = C.PS[tg % 2]
            P.op("tensor", lambda e, ofc=ofc, tg=tg, ps=ps: e.matmul(
                ps, C.WB[0:E, ofc * 128:(ofc + 1) * 128],
                C.WB[0:E, 1024 + tg * 512:1024 + (tg + 1) * 512],
                start=True, stop=True))
            P.barrier()
            if cfg["base"] == "x":
                basesl = C.XB[:, ofc * 2048 + tg * 512: ofc * 2048 + (tg + 1) * 512]
            else:
                basesl = C.YB[:, ofc * 2048 + tg * 512: ofc * 2048 + (tg + 1) * 512]
            P.op("vector", lambda e, ps=ps, basesl=basesl: e.scalar_tensor_tensor(
                C.AUX, ps, 1.0, basesl, ALU.mult, ALU.add))
            P.barrier()
            P.dma("sync", cfg["acc_d"][ofc * 128:(ofc + 1) * 128,
                                       tg * 512:(tg + 1) * 512], C.AUX)
    P.barrier()
    for ei in range(E):
        P.dma("sync", C.WF[:, 2048:4096],
              cfg["comb_d"][ei:ei + 1, :].broadcast_to([128, 2048]))
        P.barrier()
        if cfg["src"] == "sx":
            P.dma("sync", C.AUX[:, 16:24],
                  cfg["scale_d"][ei].rearrange("(c p) -> p c", p=128),
                  allow_slow_non_contiguous=True)
            P.dma("sync", C.AUX[:, 24:32],
                  cfg["bias_d"][ei].rearrange("(c p) -> p c", p=128),
                  allow_slow_non_contiguous=True)
            P.barrier()
            for c in range(8):
                P.op("vector", lambda e, c=c: e.tensor_scalar(
                    C.YB[:, c * 2048:(c + 1) * 2048],
                    C.ZB[:, c * 2048:(c + 1) * 2048],
                    C.AUX[:, 16 + c:17 + c], C.AUX[:, 24 + c:25 + c],
                    ALU.mult, ALU.add))
            P.barrier()
        P.dma("sync", C.AUX[:, 64:64 + nfc],
              cfg["b1_d"][ei].rearrange("(c p) -> p c", p=128),
              allow_slow_non_contiguous=True)
        P.barrier()
        for fc in range(nfc):
            P.dma3("sync", sb3(C.WF[:, 0:1024], 8, 128),
                   cfg["w1_d"][ei, :, fc * 128:(fc + 1) * 128]
                   .rearrange("(c p) o -> p c o", p=128))
            P.barrier()
            P.op("vector", lambda e: e.tensor_copy(C.WB[:, 0:1024],
                                                   C.WF[:, 0:1024]))
            P.barrier()
            for tg in range(4):
                ps = C.PS[tg % 2]
                for ic in range(8):
                    P.op("tensor", lambda e, ic=ic, ps=ps, tg=tg: e.matmul(
                        ps, C.WB[:, ic * 128:(ic + 1) * 128],
                        C.YB[:, ic * 2048 + tg * 512: ic * 2048 + (tg + 1) * 512],
                        start=(ic == 0), stop=(ic == 7)))
                P.barrier()
                P.op("scalar", lambda e, ps=ps, fc=fc: e.activation(
                    C.AUX2, ps, act, bias=C.AUX[:, 64 + fc:65 + fc]))
                P.barrier()
                htile = (C.WB[:, 3072:3584] if cfg["src"] == "sx"
                         else C.ZB[:, 15872:16384])
                P.op("vector", lambda e, tg=tg, htile=htile: e.tensor_mul(
                    htile, C.AUX2,
                    C.WF[:, 2048 + tg * 512:2048 + (tg + 1) * 512]))
                P.barrier()
                P.dma("sync", cfg["h_d"][fc * 128:(fc + 1) * 128,
                                         tg * 512:(tg + 1) * 512], htile)
                P.barrier()
        for ofc in range(8):
            P.dma3("sync", sb3(C.WF[:, 0:nfc * 128], nfc, 128),
                   cfg["w2_d"][ei, :, ofc * 128:(ofc + 1) * 128]
                   .rearrange("(c p) o -> p c o", p=128))
            P.barrier()
            P.op("vector", lambda e: e.tensor_copy(C.WB[:, 0:nfc * 128],
                                                   C.WF[:, 0:nfc * 128]))
            P.barrier()
            hbuf = C.WB[:, 3072:4096] if cfg["src"] == "sx" else C.ZB
            for tg in range(4):
                P.dma3("sync", sb3(hbuf[:, 0:nfc * 512], nfc, 512),
                       rearr_fm(cfg["h_d"], 0, nfc, tg * 512, (tg + 1) * 512))
                P.barrier()
                ps = C.PS[2 + tg % 2]
                for ic in range(nfc):
                    P.op("tensor", lambda e, ic=ic, ps=ps, hbuf=hbuf: e.matmul(
                        ps, C.WB[:, ic * 128:(ic + 1) * 128],
                        hbuf[:, ic * 512:(ic + 1) * 512],
                        start=(ic == 0), stop=(ic == nfc - 1)))
                acc_sl = cfg["acc_d"][ofc * 128:(ofc + 1) * 128,
                                      tg * 512:(tg + 1) * 512]
                P.dma("sync", C.AUX2, acc_sl)
                P.barrier()
                P.op("vector", lambda e, ps=ps: e.scalar_tensor_tensor(
                    C.AUX, ps, 1.0, C.AUX2, ALU.mult, ALU.add))
                P.barrier()
                P.dma("sync", acc_sl, C.AUX)
                P.barrier()


def build_l0f():
    nc = bass.Bass("TRN2", target_bir_lowering=False)
    x_d = nc.dram_tensor("xi", [D, TOK], F32, kind="ExternalInput").ap()
    wg_d = nc.dram_tensor("wg", [D, 8], F32, kind="ExternalInput").ap()
    lns_d = nc.dram_tensor("lns", [8, D], F32, kind="ExternalInput").ap()
    lnb_d = nc.dram_tensor("lnb", [8, D], F32, kind="ExternalInput").ap()
    w1_d = nc.dram_tensor("w1", [8, D, 256], F32, kind="ExternalInput").ap()
    b1_d = nc.dram_tensor("b1", [8, 256], F32, kind="ExternalInput").ap()
    w2_d = nc.dram_tensor("w2", [8, 256, D], F32, kind="ExternalInput").ap()
    b2_d = nc.dram_tensor("b2", [8, D], F32, kind="ExternalInput").ap()
    xo_d = nc.dram_tensor("xo", [D, TOK], F32, kind="ExternalOutput").ap()
    comb_d = nc.dram_tensor("comb", [8, TOK], F32, kind="Internal").ap()
    h_d = nc.dram_tensor("h_d", [256, TOK], BF16, kind="Internal").ap()

    C = Ctx(nc)
    C.emit_identities()
    emit_ln(C, x_d, TOK, None)
    emit_routing(C, wg_d, 8, 1, comb_d)
    emit_ln_core(C, lambda c: C.YB[:, c * 2048:(c + 1) * 2048], 2048,
                 lambda c: C.ZB[:, c * 2048:(c + 1) * 2048], 1e-5)
    emit_moe(C, dict(E=8, F=256, w1_d=w1_d, b1_d=b1_d, w2_d=w2_d, b2_d=b2_d,
                     comb_d=comb_d, h_d=h_d, acc_d=xo_d, act=AF.Gelu,
                     src="sx", base="x", scale_d=lns_d, bias_d=lnb_d))
    C.P.barrier()
    C.P.final_wait()
    C.P.emit()
    return nc


def build_l2f():
    nc = bass.Bass("TRN2", target_bir_lowering=False)
    x_d = nc.dram_tensor("xi", [D, TOK], F32, kind="ExternalInput").ap()
    gw_d = nc.dram_tensor("gw", [D, 8], F32, kind="ExternalInput").ap()
    w1_d = nc.dram_tensor("w1", [8, D, 4096], F32, kind="ExternalInput").ap()
    b1_d = nc.dram_tensor("b1", [8, 4096], F32, kind="ExternalInput").ap()
    w2_d = nc.dram_tensor("w2", [8, 4096, D], F32, kind="ExternalInput").ap()
    b2_d = nc.dram_tensor("b2", [8, D], F32, kind="ExternalInput").ap()
    xo_d = nc.dram_tensor("xo", [D, TOK], F32, kind="ExternalOutput").ap()
    comb_d = nc.dram_tensor("comb", [8, TOK], F32, kind="Internal").ap()
    h_d = nc.dram_tensor("h_d", [4096, TOK], BF16, kind="Internal").ap()

    C = Ctx(nc)
    C.emit_identities()
    emit_ln(C, x_d, TOK, None)
    emit_routing(C, gw_d, 8, 2, comb_d)
    emit_moe(C, dict(E=8, F=4096, w1_d=w1_d, b1_d=b1_d, w2_d=w2_d, b2_d=b2_d,
                     comb_d=comb_d, h_d=h_d, acc_d=xo_d, act=AF.Gelu,
                     src="lnx", base="x"))
    C.P.barrier()
    C.P.final_wait()
    C.P.emit()
    return nc


def build_l3f():
    nc = bass.Bass("TRN2", target_bir_lowering=False)
    x_d = nc.dram_tensor("xi", [D, TOK], F32, kind="ExternalInput").ap()
    wr_d = nc.dram_tensor("wr", [D, 8], F32, kind="ExternalInput").ap()
    br_d = nc.dram_tensor("br", [8], F32, kind="ExternalInput").ap()
    w1_d = nc.dram_tensor("w1", [8, D, 4096], F32, kind="ExternalInput").ap()
    b1_d = nc.dram_tensor("b1", [8, 4096], F32, kind="ExternalInput").ap()
    w2_d = nc.dram_tensor("w2", [8, 4096, D], F32, kind="ExternalInput").ap()
    b2_d = nc.dram_tensor("b2", [8, D], F32, kind="ExternalInput").ap()
    gam_d = nc.dram_tensor("gam", [D], F32, kind="ExternalInput").ap()
    bet_d = nc.dram_tensor("bet", [D], F32, kind="ExternalInput").ap()
    xo_d = nc.dram_tensor("xo", [D, TOK], F32, kind="ExternalOutput").ap()
    comb_d = nc.dram_tensor("comb", [8, TOK], F32, kind="Internal").ap()
    h_d = nc.dram_tensor("h_d", [4096, TOK], BF16, kind="Internal").ap()
    u_d = nc.dram_tensor("u_d", [D, TOK], F32, kind="Internal").ap()

    C = Ctx(nc)
    P = C.P
    C.emit_identities()
    emit_ln(C, x_d, TOK, None)
    emit_routing(C, wr_d, 8, 2, comb_d, br_d=br_d)
    emit_moe(C, dict(E=8, F=4096, w1_d=w1_d, b1_d=b1_d, w2_d=w2_d, b2_d=b2_d,
                     comb_d=comb_d, h_d=h_d, acc_d=u_d, act=AF.Relu,
                     src="lnx", base="lnx"))
    P.barrier()
    emit_ln(C, u_d, TOK, None, eps=1e-5, gamma_d=gam_d, beta_d=bet_d)
    P.dma3("sync", sb3(C.XB[:, 0:8 * 2048], 8, 2048), rearr_fm(x_d, 0, 8, 0, 2048))
    P.barrier()
    for c in range(8):
        P.op("vector", lambda e, c=c: e.tensor_add(
            C.WF[:, 0:2048], C.XB[:, c * 2048:(c + 1) * 2048],
            C.YB[:, c * 2048:(c + 1) * 2048]))
        P.barrier()
        P.dma("sync", xo_d[c * 128:(c + 1) * 128, :], C.WF[:, 0:2048])
        P.barrier()
    P.final_wait()
    P.emit()
    return nc


def _relayout(inputs):
    A = {}
    g = lambda k: np.asarray(inputs[k])
    A["L1f_wg"] = np.ascontiguousarray(g("L1_ffn_w_gate").T)
    A["L1f_wu"] = np.ascontiguousarray(g("L1_ffn_w_up").T)
    A["L1f_wd"] = np.ascontiguousarray(g("L1_ffn_w_down").T)
    A["L0f_wg"] = np.ascontiguousarray(g("L0_ffn_Wg").T)
    A["L0f_lns"] = np.ascontiguousarray(g("L0_ffn_ln_scale"))
    A["L0f_lnb"] = np.ascontiguousarray(g("L0_ffn_ln_bias"))
    A["L0f_w1"] = np.ascontiguousarray(g("L0_ffn_W1"))
    A["L0f_b1"] = np.ascontiguousarray(g("L0_ffn_b1"))
    A["L0f_w2"] = np.ascontiguousarray(g("L0_ffn_W2"))
    A["L0f_b2"] = np.ascontiguousarray(g("L0_ffn_b2"))
    A["L2f_gw"] = np.ascontiguousarray(g("L2_ffn_gate_w"))
    A["L2f_w1"] = np.ascontiguousarray(g("L2_ffn_w1"))
    A["L2f_b1"] = np.ascontiguousarray(g("L2_ffn_b1"))
    A["L2f_w2"] = np.ascontiguousarray(g("L2_ffn_w2"))
    A["L2f_b2"] = np.ascontiguousarray(g("L2_ffn_b2"))
    A["L3f_wr"] = np.ascontiguousarray(g("L3_ffn_Wr").T)
    A["L3f_br"] = np.ascontiguousarray(g("L3_ffn_br"))
    A["L3f_w1"] = np.ascontiguousarray(np.swapaxes(g("L3_ffn_W1"), 1, 2))
    A["L3f_b1"] = np.ascontiguousarray(g("L3_ffn_b1"))
    A["L3f_w2"] = np.ascontiguousarray(np.swapaxes(g("L3_ffn_W2"), 1, 2))
    A["L3f_b2"] = np.ascontiguousarray(g("L3_ffn_b2"))
    A["L3f_gam"] = np.ascontiguousarray(g("L3_ffn_gamma"))
    A["L3f_bet"] = np.ascontiguousarray(g("L3_ffn_beta"))
    A["L3a_wqkv"] = np.ascontiguousarray(g("L3_attn_w_qkv"))
    A["L3a_qg"] = np.ascontiguousarray(g("L3_attn_q_gamma"))
    A["L3a_qb"] = np.ascontiguousarray(g("L3_attn_q_beta"))
    A["L3a_kg"] = np.ascontiguousarray(g("L3_attn_k_gamma"))
    A["L3a_kb"] = np.ascontiguousarray(g("L3_attn_k_beta"))
    A["L3a_wp"] = np.ascontiguousarray(g("L3_attn_w_proj"))
    A["L3a_bp"] = np.ascontiguousarray(g("L3_attn_b_proj"))
    A["L0a_wq"] = np.ascontiguousarray(g("L0_attn_Wq").reshape(1024, 1024))
    A["L0a_wkv"] = np.ascontiguousarray(g("L0_attn_Wkv_down"))
    A["L0a_wku"] = np.ascontiguousarray(g("L0_attn_Wk_up").reshape(256, 1024))
    A["L0a_wvu"] = np.ascontiguousarray(g("L0_attn_Wv_up").reshape(256, 1024))
    A["L0a_wo"] = np.ascontiguousarray(g("L0_attn_Wout").reshape(1024, 1024))
    A["L2a_wq"] = np.ascontiguousarray(g("L2_attn_w_q").T)
    A["L2a_wk"] = np.ascontiguousarray(g("L2_attn_w_k").T)
    A["L2a_wv"] = np.ascontiguousarray(g("L2_attn_w_v").T)
    A["L2a_wo"] = np.ascontiguousarray(g("L2_attn_w_o").T)
    A["L2a_gate"] = np.ascontiguousarray(g("L2_attn_gate"))
    A["L2a_mem"] = np.ascontiguousarray(g("L2_attn_memory"))
    A["L2a_mn"] = np.ascontiguousarray(g("L2_attn_memory_norm"))
    A["L1a_wq"] = np.ascontiguousarray(g("L1_attn_Wq"))
    A["L1a_wk"] = np.ascontiguousarray(g("L1_attn_Wk"))
    A["L1a_wv"] = np.ascontiguousarray(g("L1_attn_Wv"))
    A["L1a_wo"] = np.ascontiguousarray(g("L1_attn_Wo"))
    A["L1a_omt"] = np.ascontiguousarray(g("L1_attn_omega").T)
    A["L1a_w1"] = np.ascontiguousarray(g("L1_attn_W1"))
    A["L1a_b1"] = np.ascontiguousarray(g("L1_attn_b1"))
    A["L1a_w2"] = np.ascontiguousarray(g("L1_attn_W2"))
    A["L1a_b2"] = np.ascontiguousarray(g("L1_attn_b2"))
    A["L1a_gam"] = np.ascontiguousarray(g("L1_attn_gamma"))
    A["L1a_bet"] = np.ascontiguousarray(g("L1_attn_beta"))
    return A


_NC_CACHE = {}


def _get_nc(name, build_fn):
    if name not in _NC_CACHE:
        _NC_CACHE[name] = build_fn()
    return _NC_CACHE[name]


def _x_shards(x):
    out = []
    for c in range(8):
        b, h = c // 2, c % 2
        out.append(np.ascontiguousarray(x[b, h * 2048:(h + 1) * 2048, :].T))
    return out

def _x_unshard(shards):
    x = np.empty((4, 4096, 1024), np.float32)
    for c in range(8):
        b, h = c // 2, c % 2
        x[b, h * 2048:(h + 1) * 2048, :] = shards[c].T
    return x


def in_map_for(block, A, shards, c):
    if block == "l1f":
        return {"xi": shards[c], "wg": A["L1f_wg"], "wu": A["L1f_wu"],
                "wd": A["L1f_wd"]}
    if block == "l0f":
        return {"xi": shards[c], "wg": A["L0f_wg"], "lns": A["L0f_lns"],
                "lnb": A["L0f_lnb"], "w1": A["L0f_w1"], "b1": A["L0f_b1"],
                "w2": A["L0f_w2"], "b2": A["L0f_b2"]}
    if block == "l2f":
        return {"xi": shards[c], "gw": A["L2f_gw"], "w1": A["L2f_w1"],
                "b1": A["L2f_b1"], "w2": A["L2f_w2"], "b2": A["L2f_b2"]}
    if block == "l0a":
        return {"xf": None, "wq": A["L0a_wq"], "wkv": A["L0a_wkv"],
                "wku": A["L0a_wku"], "wvu": A["L0a_wvu"], "wo": A["L0a_wo"]}
    if block == "l1a":
        return {"xf": None, "wq": A["L1a_wq"], "wk": A["L1a_wk"],
                "wv": A["L1a_wv"], "wo": A["L1a_wo"], "omt": A["L1a_omt"],
                "w1": A["L1a_w1"], "b1": A["L1a_b1"], "w2": A["L1a_w2"],
                "b2": A["L1a_b2"], "gam": A["L1a_gam"], "bet": A["L1a_bet"]}
    if block == "l2a":
        return {"xf": None, "wq": A["L2a_wq"], "wk": A["L2a_wk"],
                "wv": A["L2a_wv"], "wo": A["L2a_wo"], "gate": A["L2a_gate"],
                "mem": A["L2a_mem"], "mn": A["L2a_mn"]}
    if block == "l3a":
        return {"xf": None, "wqkv": A["L3a_wqkv"], "qg": A["L3a_qg"],
                "qb": A["L3a_qb"], "kg": A["L3a_kg"], "kb": A["L3a_kb"],
                "wp": A["L3a_wp"], "bp": A["L3a_bp"]}
    if block == "l3f":
        return {"xi": shards[c], "wr": A["L3f_wr"], "br": A["L3f_br"],
                "w1": A["L3f_w1"], "b1": A["L3f_b1"], "w2": A["L3f_w2"],
                "b2": A["L3f_b2"], "gam": A["L3f_gam"], "bet": A["L3f_bet"]}
    raise KeyError(block)


def run_block(block, x, A):
    nc = _get_nc(block, globals()[f"build_{block}"])
    shards = _x_shards(x)
    in_maps = [in_map_for(block, A, shards, c) for c in range(8)]
    res = run_bass_kernel_spmd(nc, in_maps, core_ids=list(range(8)))
    return _x_unshard([res.results[c]["xo"] for c in range(8)])


def run_l1f(x, A):
    return run_block("l1f", x, A)


def run_l0f(x, A):
    return run_block("l0f", x, A)


def run_l2f(x, A):
    return run_block("l2f", x, A)


def run_l3f(x, A):
    return run_block("l3f", x, A)


def _xf_shards(x):
    out = []
    for c in range(8):
        b, h = c // 2, c % 2
        own = x[b, h * 2048:(h + 1) * 2048, :].T
        oth = x[b, (1 - h) * 2048:(2 - h) * 2048, :].T
        out.append(np.ascontiguousarray(np.concatenate([own, oth], axis=1)))
    return out


def _xf_natural(x):
    return [np.ascontiguousarray(x[c // 2].T) for c in range(8)]


def run_half_block(block, x, A, build_fn):
    xf = _xf_natural(x)
    outs = [None] * 8
    for half in (0, 1):
        nc = _get_nc(f"{block}{half}", lambda h=half: build_fn(h))
        cores = [2 * b + half for b in range(4)]
        in_maps = []
        for c in cores:
            m = in_map_for(block, A, None, c)
            m["xf"] = xf[c]
            in_maps.append(m)
        res = run_bass_kernel_spmd(nc, in_maps, core_ids=cores)
        for i, c in enumerate(cores):
            outs[c] = res.results[i]["xo"]
    return _x_unshard(outs)


def run_l0a(x, A):
    return run_half_block("l0a", x, A, build_l0a)


def run_l2a(x, A):
    return run_half_block("l2a", x, A, build_l2a)


def run_l1a(x, A):
    return run_half_block("l1a", x, A, build_l1a)


def make_in_maps(block, x, A):
    if block in ("l0a", "l2a", "l1a"):
        xf = _xf_natural(x)
        maps = []
        for c in range(8):
            m = in_map_for(block, A, None, c)
            m["xf"] = xf[c]
            maps.append(m)
        return maps
    if block in ("l3a",):
        xf = _xf_shards(x)
        maps = []
        for c in range(8):
            m = in_map_for(block, A, None, c)
            m["xf"] = xf[c]
            maps.append(m)
        return maps
    shards = _x_shards(x)
    return [in_map_for(block, A, shards, c) for c in range(8)]


def run_l3a(x, A):
    nc = _get_nc("l3a", build_l3a)
    res = run_bass_kernel_spmd(nc, make_in_maps("l3a", x, A),
                               core_ids=list(range(8)))
    return _x_unshard([res.results[c]["xo"] for c in range(8)])


def kernel(**inputs):
    A = _relayout(inputs)
    x = np.ascontiguousarray(np.asarray(inputs["L0_attn_x_bld"], np.float32))
    x = run_l0a(x, A)
    x = run_l0f(x, A)
    x = run_l1a(x, A)
    x = run_l1f(x, A)
    x = run_l2a(x, A)
    x = run_l2f(x, A)
    x = run_l3a(x, A)
    x = run_l3f(x, A)
    return x
